# version 01
# baseline (speedup 1.0000x reference)
import numpy as np
import ml_dtypes
from contextlib import ExitStack

import concourse.bass as bass
import concourse.mybir as mybir
from concourse.bass_utils import run_bass_kernel_spmd

F32 = mybir.dt.float32
BF16 = mybir.dt.bfloat16
ALU = mybir.AluOpType
AF = mybir.ActivationFunctionType
AX = mybir.AxisListType

D = 1024
T = 4096
R = 256
NG = 8
TG = 512
NSUB = 4
NDC = 8
NEXP = 10
NRING = 15
SLOTW = 2048
EPS = 1e-5


class Ctr:
    def __init__(self, nc, es, name):
        self.h = es.enter_context(nc.semaphore(name))
        self.v = 0

    def bump(self, n=1):
        self.v += n
        return self.v


class Q(list):

    def __init__(self, ctr=None):
        super().__init__()
        self.ctr = ctr

    def append(self, fn, is_wait=False):
        if self.ctr is not None and not is_wait:
            v = self.ctr.v
            if v > 0:
                list.append(self, lambda e, s=self.ctr.h, vv=v: e.wait_ge(s, vv))
            self.ctr.bump()
            list.append(self, (lambda o, s=self.ctr.h:
                               (lambda e: o(e).then_inc(s, 1)))(fn))
        else:
            list.append(self, fn)


class Qs:
    def __init__(self, dve=None, act=None):
        self.tensor = Q()
        self.vector = Q(dve)
        self.scalar = Q(act)
        self.sync = Q()
        self.gpsimd = Q()


def _w(q, ctr, val):
    if val > 0:
        q.append(lambda e, s=ctr.h, v=val: e.wait_ge(s, v), is_wait=True)


def _inc_last(q, ctr, n=1):
    if getattr(q, "ctr", None) is ctr:
        return ctr.v
    v = ctr.bump(n)
    old = q[-1]
    q[-1] = (lambda o, s=ctr.h, nn=n: (lambda e: o(e).then_inc(s, nn)))(old)
    return v


class Ring:

    def __init__(self, nc, es, qs, pe):
        self.slots = [es.enter_context(nc.sbuf_tensor(f"ring{i}", [128, SLOTW], BF16))
                      for i in range(NRING)]
        self.sems = [Ctr(nc, es, f"rg{i}") for i in range(NRING)]
        self.free_at = [0] * NRING
        self.loaded = [0] * NRING
        self.qs = qs
        self.pe = pe
        self.next = 0

    def alloc(self):
        i = self.next
        self.next = (self.next + 1) % NRING
        return i

    def load(self, src_ap, n, split=None):
        i = self.alloc()
        _w(self.qs.sync, self.pe, self.free_at[i])
        v = self.sems[i].bump(16)
        slot = self.slots[i]
        dst = slot[:, 0:n]
        if split is not None:
            dst = dst.rearrange("p (c i) -> p c i", c=split)
        self.qs.sync.append(
            lambda e, s=dst, a=src_ap, sem=self.sems[i].h:
            e.dma_start(out=s, in_=a).then_inc(sem, 16))
        self.loaded[i] = v
        return i

    def wait_loaded(self, q, i):
        _w(q, self.sems[i], self.loaded[i])

    def release(self, i, pe_val):
        self.free_at[i] = max(self.free_at[i], pe_val)


class Psum:
    def __init__(self, nc, es):
        self.banks = [es.enter_context(nc.psum_tensor(f"pb{i}", [128, 512], F32))
                      for i in range(8)]
        self.free = [None] * 8

    def acquire(self, q, i):
        f = self.free[i]
        if f is not None:
            _w(q, f[0], f[1])
            self.free[i] = None
        return self.banks[i]

    def set_free(self, i, ctr, val):
        self.free[i] = (ctr, val)


def build_program(debug=(), stage="all", ngroups=NG):
    nc = bass.Bass("TRN2", target_bir_lowering=False, debug=False)
    es = ExitStack()
    dbg = {}

    def dramin(name, shape, dt=F32):
        return nc.dram_tensor(name, list(shape), dt, kind="ExternalInput").ap()

    def dramout(name, shape, dt=F32):
        return nc.dram_tensor(name, list(shape), dt, kind="ExternalOutput").ap()

    def dscr(name, shape, dt=BF16):
        return nc.dram_tensor(name, list(shape), dt, kind="Internal").ap()

    x_tm = dramin("x_tm", [T, D])
    x_fm = dramin("x_fm", [D, T])
    Wq = dramin("Wq", [D, D])
    Wk = dramin("Wk", [D, D])
    WoT = dramin("WoT", [D, D])
    V_b = dramin("V_b", [D, R]); V_t = dramin("V_t", [D, R])
    W_b = dramin("W_b", [D, R]); W_t = dramin("W_t", [D, R])
    X_t = dramin("X_t", [D, R])
    U_b = dramin("U_b", [D, R]); U_t = dramin("U_t", [D, R])
    bias_cols = dramin("bias_cols", [128, 16])
    alpha_col = dramin("alpha_col", [128, 1])
    gwT = dramin("gwT", [D, 8])
    gbias_bc = dramin("gbias_bc", [128, 8])
    w1T = dramin("w1T", [8 * D, D])
    w3T = dramin("w3T", [8 * D, D])
    w2T = dramin("w2T", [8 * D, D])
    sw1T = dramin("sw1T", [D, 2 * D])
    sw3T = dramin("sw3T", [D, 2 * D])
    sw2T = dramin("sw2T", [2 * D, D])
    ident = dramin("ident", [128, 128])
    counts = dramin("counts", [128, 32])

    out = dramout("out", [T, D])

    for tag, shape, dt in [
        ("cC_hi", [D, 1280], BF16), ("cC_lo", [D, 1280], BF16),
        ("cU_hi", [2 * R, D], BF16), ("cU_lo", [2 * R, D], BF16),
        ("bias_bc", [128, D], F32),
        ("P", [10 * 128, TG], F32),
        ("x1", [TG, D], F32),
        ("gates", [TG, 8], F32),
        ("r2", [TG, D], F32),
        ("acc", [NEXP * TG, D], F32),
        ("h0", [16 * 128, TG], BF16),
    ]:
        if tag in debug:
            dbg[tag] = dramout("dbg_" + tag, shape, dt)

    cC_hi = dscr("cC_hi", [D, 1280])
    cC_lo = dscr("cC_lo", [D, 1280])
    cU_hi = dscr("cU_hi", [2 * R, D])
    cU_lo = dscr("cU_lo", [2 * R, D])
    wS_hi = dscr("wS_hi", [NEXP * 3 * D, D])
    wS_lo = dscr("wS_lo", [NEXP * 3 * D, D])

    with es:
        sb = lambda n, s, dt=F32: es.enter_context(nc.sbuf_tensor(n, s, dt))

        pe = Ctr(nc, es, "pe")
        dve = Ctr(nc, es, "dve")
        act = Ctr(nc, es, "act")
        qs = Qs(dve, act)
        gpc = Ctr(nc, es, "gpc")
        stg_sem = [Ctr(nc, es, f"stg{i}") for i in range(2)]
        xfm_sem = [Ctr(nc, es, f"xfm{i}") for i in range(2)]
        xtm_sem = [Ctr(nc, es, f"xtm{i}") for i in range(2)]
        st_sem = [Ctr(nc, es, f"st{i}") for i in range(2)]
        wsp_sem = [Ctr(nc, es, f"wsp{i}") for i in range(2)]
        wev_sem = Ctr(nc, es, "wev")
        dbg_sem = Ctr(nc, es, "dbg")
        sml_sem = Ctr(nc, es, "sml")

        ring = Ring(nc, es, qs, pe)
        ps = Psum(nc, es)

        stgbuf = [sb(f"stage{i}", [128, D]) for i in range(2)]
        hi_buf = [sb(f"hib{i}", [128, D], BF16) for i in range(2)]
        lo_buf = [sb(f"lob{i}", [128, D], BF16) for i in range(2)]
        efp = sb("efp", [128, 512])
        ehi = sb("ehi", [128, 512], BF16)
        elo = sb("elo", [128, 512], BF16)
        xfm_stage = [stgbuf[0][:, 0:TG], stgbuf[1][:, 0:TG]]
        xfmh = sb("xfmh", [128, NDC * TG], BF16)
        xfml = sb("xfml", [128, NDC * TG], BF16)
        xtm_b = [sb(f"xtmb{i}", [128, D]) for i in range(2)]
        P = [sb(f"P{j}", [128, TG]) for j in range(10)]
        barena = [sb(f"ba{j}", [128, TG], BF16) for j in range(16)]
        x1 = [sb(f"x1_{s}", [128, D]) for s in range(NSUB)]
        x1h = sb("x1h", [128, NDC * TG], BF16)
        x1l = sb("x1l", [128, NDC * TG], BF16)
        outb = [sb(f"outb{i}", [128, D]) for i in range(2)]
        sqs = sb("sqs", [128, D], BF16)
        silu = sb("silu", [128, TG])
        cnt = sb("cnt", [128, 32])
        icnt = sb("icnt", [128, 32])
        icnt2 = sb("icnt2", [128, 32])
        idn = sb("idn", [128, 128])
        alph = sb("alph", [128, 1])
        bcols = sb("bcols", [128, 16])
        bcomb = sb("bcomb", [128, 8])
        bcomb_h = sb("bcomb_h", [128, 8], BF16)
        bcomb_l = sb("bcomb_l", [128, 8], BF16)
        ones_r = sb("ones_r", [1, 128], BF16)
        bias_bc = sb("bias_bc", [128, D])
        gwh = sb("gwh", [128, 64], BF16)
        gwl = sb("gwl", [128, 64], BF16)
        gbias = sb("gbias", [128, 8])
        carry = sb("carry", [128, 8])
        gates = sb("gates", [128, 32])
        lgfm = sb("lgfm", [8, TG])
        stats = sb("stats", [128, 16])
        xb = [sb(f"xb{i}", [128, D]) for i in range(2)]
        expv = sb("expv", [128, 64])
        brow = xb[0][0:1, :]
        brow_h = hi_buf[0][0:1, :]
        brow_l = lo_buf[0][0:1, :]

        for dst, src in [(cnt, counts), (idn, ident), (alph, alpha_col),
                         (bcols, bias_cols), (gbias, gbias_bc)]:
            qs.sync.append(lambda e, d=dst, s=src, h=sml_sem.h:
                           e.dma_start(out=d[:], in_=s).then_inc(h, 16))
            sml_sem.bump(16)
        sml_target = sml_sem.v

        _w(qs.vector, sml_sem, sml_target)
        qs.vector.append(lambda e: e.memset(carry[:], 0.0))
        qs.vector.append(lambda e: e.reciprocal(out=icnt[:], in_=cnt[:]))
        qs.vector.append(lambda e: e.tensor_tensor(out=icnt2[:], in0=icnt[:],
                                                   in1=icnt[:], op=ALU.mult))
        qs.vector.append(lambda e: e.scalar_tensor_tensor(
            out=bcomb[:], in0=bcols[:, 8:16], scalar=alph[:, 0:1],
            in1=bcols[:, 0:8], op0=ALU.mult, op1=ALU.add))
        qs.vector.append(lambda e: e.tensor_copy(out=bcomb_h[:], in_=bcomb[:]))
        qs.vector.append(lambda e: e.tensor_tensor(out=bcomb_l[:], in0=bcomb[:],
                                                   in1=bcomb_h[:], op=ALU.subtract))
        qs.vector.append(lambda e: e.memset(ones_r[:], 1.0))
        v_pro = _inc_last(qs.vector, dve)

        class Splitter:

            def __init__(self):
                self.i = 0
                self.guards = [[], []]

            def split(self, src_ap, n):
                i = self.i % 2
                self.i += 1
                sem = stg_sem[i]
                for (c, v) in self.guards[i]:
                    _w(qs.sync, c, v)
                self.guards[i] = []
                v = sem.bump(16)
                qs.sync.append(lambda e, d=stgbuf[i], s=src_ap, nn=n, w=sem.h:
                               e.dma_start(out=d[:, 0:nn], in_=s).then_inc(w, 16))
                _w(qs.scalar, sem, v)
                qs.scalar.append(lambda e, d=hi_buf[i], s=stgbuf[i], nn=n:
                                 e.activation(out=d[:, 0:nn], in_=s[:, 0:nn],
                                              func=AF.Copy))
                av = _inc_last(qs.scalar, act)
                _w(qs.vector, sem, v)
                _w(qs.vector, act, av)
                qs.vector.append(lambda e, d=lo_buf[i], s=stgbuf[i], h=hi_buf[i], nn=n:
                                 e.tensor_tensor(out=d[:, 0:nn], in0=s[:, 0:nn],
                                                 in1=h[:, 0:nn], op=ALU.subtract))
                dv = _inc_last(qs.vector, dve)
                self.guards[i].append((act, av))
                self.guards[i].append((dve, dv))
                return hi_buf[i], lo_buf[i], av, dv, i

            def guard(self, i, ctr, val):
                self.guards[i].append((ctr, val))

        sp = Splitter()

        def load_small_hl(mats):
            width = R * len(mats)
            per_slot = SLOTW // width
            nslot = (NDC + per_slot - 1) // per_slot
            his = [ring.alloc() for _ in range(nslot)]
            los = [ring.alloc() for _ in range(nslot)]
            ready = []
            for c in range(NDC):
                s_i = c // per_slot
                off = (c % per_slot) * width
                for j, m in enumerate(mats):
                    hb, lb, av, dv, bi = sp.split(m[c * 128:(c + 1) * 128, :], R)
                    hs = ring.slots[his[s_i]]
                    ls = ring.slots[los[s_i]]
                    c0 = off + j * R
                    qs.scalar.append(lambda e, d=hs, s=hb, cc=c0:
                                     e.activation(out=d[:, cc:cc + R], in_=s[:, 0:R],
                                                  func=AF.Copy))
                    av2 = _inc_last(qs.scalar, act)
                    _w(qs.vector, act, av2)
                    qs.vector.append(lambda e, d=ls, s=lb, cc=c0:
                                     e.tensor_copy(out=d[:, cc:cc + R], in_=s[:, 0:R]))
                    dv2 = _inc_last(qs.vector, dve)
                    sp.guard(bi, act, av2)
                    sp.guard(bi, dve, dv2)
                    ready.append((av2, dv2))

            def get(slots, c):
                s_i = c // per_slot
                off = (c % per_slot) * width
                return ring.slots[slots[s_i]][:, off:off + width]

            return (lambda c: get(his, c)), (lambda c: get(los, c)), ready, his + los

        def mm3(pb_ap, lh, ll, rh, rl, start, stop):
            qs.tensor.append(lambda e, p=pb_ap, a=lh, b=rh, s=start:
                             e.matmul(p, a, b, start=s, stop=False))
            qs.tensor.append(lambda e, p=pb_ap, a=lh, b=rl:
                             e.matmul(p, a, b, start=False, stop=False))
            qs.tensor.append(lambda e, p=pb_ap, a=ll, b=rh, s=stop:
                             e.matmul(p, a, b, start=False, stop=s))

        def evac_to_scratch(pb, w, dsthi, dstlo, pv, scale_alpha=False):
            _w(qs.vector, pe, pv)
            _w(qs.vector, wev_sem, wev_sem.v)
            if scale_alpha:
                qs.vector.append(lambda e, p=pb, ww=w:
                                 e.tensor_scalar(out=efp[:, 0:ww], in0=p[:, 0:ww],
                                                 scalar1=alph[:, 0:1], scalar2=None,
                                                 op0=ALU.mult))
            else:
                qs.vector.append(lambda e, p=pb, ww=w:
                                 e.tensor_copy(out=efp[:, 0:ww], in_=p[:, 0:ww]))
            dv0 = _inc_last(qs.vector, dve)
            _w(qs.scalar, dve, dv0)
            qs.scalar.append(lambda e, ww=w:
                             e.activation(out=ehi[:, 0:ww], in_=efp[:, 0:ww],
                                          func=AF.Copy))
            av = _inc_last(qs.scalar, act)
            _w(qs.vector, act, av)
            qs.vector.append(lambda e, ww=w:
                             e.tensor_tensor(out=elo[:, 0:ww], in0=efp[:, 0:ww],
                                             in1=ehi[:, 0:ww], op=ALU.subtract))
            dv = _inc_last(qs.vector, dve)
            _w(qs.sync, act, av)
            qs.sync.append(lambda e, d=dsthi, ww=w:
                           e.dma_start(out=d, in_=ehi[:, 0:ww]).then_inc(wev_sem.h, 16))
            wev_sem.bump(16)
            _w(qs.sync, dve, dv)
            qs.sync.append(lambda e, d=dstlo, ww=w:
                           e.dma_start(out=d, in_=elo[:, 0:ww]).then_inc(wev_sem.h, 16))
            wev_sem.bump(16)
            return dv0

        for (big, cols0, mats) in [(Wq, 0, [V_b, V_t]), (Wk, 512, [W_b, W_t])]:
            get_hi, get_lo, ready, used = load_small_hl(mats)
            banks = [ps.acquire(qs.tensor, i) for i in range(8)]
            _w(qs.tensor, act, max(v[0] for v in ready))
            _w(qs.tensor, dve, max(v[1] for v in ready))
            for c in range(NDC):
                hb, lb, av, dv, bi = sp.split(big[c * 128:(c + 1) * 128, :], D)
                _w(qs.tensor, act, av)
                _w(qs.tensor, dve, dv)
                for dch in range(NDC):
                    mm3(banks[dch][:], hb[:, dch * 128:(dch + 1) * 128],
                        lb[:, dch * 128:(dch + 1) * 128],
                        get_hi(c), get_lo(c), c == 0, c == NDC - 1)
                pv = _inc_last(qs.tensor, pe)
                sp.guard(bi, pe, pv)
            pv = pe.v
            for dch in range(NDC):
                dv0 = evac_to_scratch(
                    banks[dch], 512,
                    cC_hi[dch * 128:(dch + 1) * 128, cols0:cols0 + 512],
                    cC_lo[dch * 128:(dch + 1) * 128, cols0:cols0 + 512], pv)
                ps.set_free(dch, dve, dv0)
            for i in used:
                ring.release(i, pv)

        for c in range(NDC):
            hb, lb, av, dv, bi = sp.split(X_t[c * 128:(c + 1) * 128, :], R)
            ws = wsp_sem[bi]
            _w(qs.sync, act, av)
            qs.sync.append(lambda e, s=hb, d=cC_hi[c * 128:(c + 1) * 128, 1024:1280],
                           w=ws.h:
                           e.dma_start(out=d, in_=s[:, 0:R]).then_inc(w, 16))
            ws.bump(16)
            _w(qs.sync, dve, dv)
            qs.sync.append(lambda e, s=lb, d=cC_lo[c * 128:(c + 1) * 128, 1024:1280],
                           w=ws.h:
                           e.dma_start(out=d, in_=s[:, 0:R]).then_inc(w, 16))
            ws.bump(16)
            sp.guard(bi, ws, ws.v)

        get_hi, get_lo, ready, used = load_small_hl([U_b, U_t])
        banks = [ps.acquire(qs.tensor, i) for i in range(8)]
        _w(qs.tensor, act, max(v[0] for v in ready))
        _w(qs.tensor, dve, max(v[1] for v in ready))
        for c in range(NDC):
            hb, lb, av, dv, bi = sp.split(WoT[c * 128:(c + 1) * 128, :], D)
            _w(qs.tensor, act, av)
            _w(qs.tensor, dve, dv)
            for rc in range(4):
                uh = get_hi(c)[:, rc * 128:(rc + 1) * 128]
                ul = get_lo(c)[:, rc * 128:(rc + 1) * 128]
                for dh in range(2):
                    mm3(banks[rc * 2 + dh][:], uh, ul,
                        hb[:, dh * 512:(dh + 1) * 512],
                        lb[:, dh * 512:(dh + 1) * 512], c == 0, c == NDC - 1)
            pv = _inc_last(qs.tensor, pe)
            sp.guard(bi, pe, pv)
        pv = pe.v
        for rc in range(4):
            for dh in range(2):
                dv0 = evac_to_scratch(
                    banks[rc * 2 + dh], 512,
                    cU_hi[rc * 128:(rc + 1) * 128, dh * 512:(dh + 1) * 512],
                    cU_lo[rc * 128:(rc + 1) * 128, dh * 512:(dh + 1) * 512],
                    pv, scale_alpha=(rc >= 2))
                ps.set_free(rc * 2 + dh, dve, dv0)
        for i in used:
            ring.release(i, pv)

        b0 = ps.acquire(qs.tensor, 0)
        b1 = ps.acquire(qs.tensor, 1)
        _w(qs.tensor, dve, v_pro)
        for c in range(NDC):
            hb, lb, av, dv, bi = sp.split(WoT[c * 128:(c + 1) * 128, :], D)
            _w(qs.tensor, act, av)
            _w(qs.tensor, dve, dv)
            for dh in range(2):
                mm3([b0, b1][dh][0:1, :], bcomb_h[:, c:c + 1], bcomb_l[:, c:c + 1],
                    hb[:, dh * 512:(dh + 1) * 512], lb[:, dh * 512:(dh + 1) * 512],
                    c == 0, c == NDC - 1)
            pv = _inc_last(qs.tensor, pe)
            sp.guard(bi, pe, pv)
        pvB = pe.v
        _w(qs.vector, pe, pvB)
        qs.vector.append(lambda e: e.tensor_copy(out=brow[0:1, 0:512], in_=b0[0:1, :]))
        qs.vector.append(lambda e: e.tensor_copy(out=brow[0:1, 512:1024], in_=b1[0:1, :]))
        qs.vector.append(lambda e: e.tensor_copy(out=brow_h, in_=brow))
        qs.vector.append(lambda e: e.tensor_tensor(out=brow_l, in0=brow,
                                                   in1=brow_h, op=ALU.subtract))
        dvb = _inc_last(qs.vector, dve)
        ps.set_free(0, dve, dvb)
        ps.set_free(1, dve, dvb)
        b0 = ps.acquire(qs.tensor, 0)
        b1 = ps.acquire(qs.tensor, 1)
        _w(qs.tensor, dve, dvb)
        for dh in range(2):
            pb = [b0, b1][dh]
            qs.tensor.append(lambda e, p=pb, d=dh:
                             e.matmul(p[:], ones_r[0:1, :],
                                      brow_h[0:1, d * 512:(d + 1) * 512],
                                      start=True, stop=False))
            qs.tensor.append(lambda e, p=pb, d=dh:
                             e.matmul(p[:], ones_r[0:1, :],
                                      brow_l[0:1, d * 512:(d + 1) * 512],
                                      start=False, stop=True))
        pvBB = _inc_last(qs.tensor, pe)
        sp.guard(0, pe, pvBB)
        _w(qs.vector, pe, pvBB)
        qs.vector.append(lambda e: e.tensor_copy(out=bias_bc[:, 0:512], in_=b0[:]))
        qs.vector.append(lambda e: e.tensor_copy(out=bias_bc[:, 512:1024], in_=b1[:]))
        dvb2 = _inc_last(qs.vector, dve)
        ps.set_free(0, dve, dvb2)
        ps.set_free(1, dve, dvb2)

        for c in range(NDC):
            hb, lb, av, dv, bi = sp.split(gwT[c * 128:(c + 1) * 128, :], 8)
            qs.scalar.append(lambda e, s=hb, c0=c * 8:
                             e.activation(out=gwh[:, c0:c0 + 8], in_=s[:, 0:8],
                                          func=AF.Copy))
            av2 = _inc_last(qs.scalar, act)
            _w(qs.vector, act, av2)
            qs.vector.append(lambda e, s=lb, c0=c * 8:
                             e.tensor_copy(out=gwl[:, c0:c0 + 8], in_=s[:, 0:8]))
            dv2 = _inc_last(qs.vector, dve)
            sp.guard(bi, act, av2)
            sp.guard(bi, dve, dv2)

        def row(ex, m):
            return (ex * 3 + m) * D

        def ffn_split(src_ap, ex, m):
            for c in range(NDC):
                hb, lb, av, dv, bi = sp.split(src_ap[c * 128:(c + 1) * 128, :], D)
                ws = wsp_sem[bi]
                r0 = row(ex, m) + c * 128
                _w(qs.sync, act, av)
                qs.sync.append(lambda e, s=hb, d=wS_hi[r0:r0 + 128, :], w=ws.h:
                               e.dma_start(out=d, in_=s[:, 0:D]).then_inc(w, 16))
                ws.bump(16)
                _w(qs.sync, dve, dv)
                qs.sync.append(lambda e, s=lb, d=wS_lo[r0:r0 + 128, :], w=ws.h:
                               e.dma_start(out=d, in_=s[:, 0:D]).then_inc(w, 16))
                ws.bump(16)
                sp.guard(bi, ws, ws.v)

        for half in range(2):
            ffn_split(sw1T[:, half * D:(half + 1) * D], half, 0)
            ffn_split(sw3T[:, half * D:(half + 1) * D], half, 1)
            ffn_split(sw2T[half * D:(half + 1) * D, :], half, 2)
        for ex in range(8):
            ffn_split(w1T[ex * D:(ex + 1) * D, :], ex + 2, 0)
            ffn_split(w3T[ex * D:(ex + 1) * D, :], ex + 2, 1)
            ffn_split(w2T[ex * D:(ex + 1) * D, :], ex + 2, 2)

        if dbg:
            _w(qs.sync, wsp_sem[0], wsp_sem[0].v)
            _w(qs.sync, wsp_sem[1], wsp_sem[1].v)
            _w(qs.sync, wev_sem, wev_sem.v)
            _w(qs.sync, dve, dvb2)
            for (tag, src) in [("cC_hi", cC_hi), ("cC_lo", cC_lo),
                               ("cU_hi", cU_hi), ("cU_lo", cU_lo)]:
                if tag in dbg:
                    qs.sync.append(lambda e, d=dbg[tag], s=src:
                                   e.dma_start(out=d, in_=s).then_inc(dbg_sem.h, 16))
                    dbg_sem.bump(16)
            if "bias_bc" in dbg:
                qs.sync.append(lambda e, d=dbg["bias_bc"]:
                               e.dma_start(out=d, in_=bias_bc[:]).then_inc(dbg_sem.h, 16))
                dbg_sem.bump(16)

        _w(qs.sync, wsp_sem[0], wsp_sem[0].v)
        _w(qs.sync, wsp_sem[1], wsp_sem[1].v)
        _w(qs.sync, wev_sem, wev_sem.v)
        _w(qs.sync, act, act.v)
        _w(qs.sync, dve, dve.v)
        _w(qs.sync, pe, pe.v)

        xfm_use = [None, None]
        xtm_use = [0, 0]
        xb_used = [0, 0]
        out_use = [0, 0]

        if stage == "prologue":
            ngroups = 0
        for g in range(ngroups):
            col_g = slice(g * TG, (g + 1) * TG)

            pe_bar = pe.v
            _w(qs.scalar, pe, pe_bar)
            _w(qs.vector, pe, pe_bar)

            xs_av, xs_dv = [], []
            for c in range(NDC):
                i = c % 2
                sem = xfm_sem[i]
                if xfm_use[i]:
                    _w(qs.sync, act, xfm_use[i][0])
                    _w(qs.sync, dve, xfm_use[i][1])
                v = sem.bump(16)
                qs.sync.append(lambda e, d=xfm_stage[i],
                               s=x_fm[c * 128:(c + 1) * 128, col_g], w=sem.h:
                               e.dma_start(out=d, in_=s).then_inc(w, 16))
                _w(qs.scalar, sem, v)
                qs.scalar.append(lambda e, s=xfm_stage[i], c0=c * TG:
                                 e.activation(out=xfmh[:, c0:c0 + TG], in_=s,
                                              func=AF.Copy))
                av = _inc_last(qs.scalar, act)
                _w(qs.vector, sem, v)
                _w(qs.vector, act, av)
                qs.vector.append(lambda e, s=xfm_stage[i], c0=c * TG:
                                 e.tensor_tensor(out=xfml[:, c0:c0 + TG], in0=s,
                                                 in1=xfmh[:, c0:c0 + TG],
                                                 op=ALU.subtract))
                dv = _inc_last(qs.vector, dve)
                xfm_use[i] = (av, dv)
                xs_av.append(av)
                xs_dv.append(dv)

            for half, rcs in enumerate([range(0, 5), range(5, 10)]):
                banks = [ps.acquire(qs.tensor, j - rcs[0]) for j in rcs]
                if half == 0:
                    _w(qs.tensor, act, max(xs_av))
                    _w(qs.tensor, dve, max(xs_dv))
                for c in range(NDC):
                    hi_i = ring.load(cC_hi[c * 128:(c + 1) * 128, :], 1280)
                    lo_i = ring.load(cC_lo[c * 128:(c + 1) * 128, :], 1280)
                    ring.wait_loaded(qs.tensor, hi_i)
                    ring.wait_loaded(qs.tensor, lo_i)
                    hs, ls = ring.slots[hi_i], ring.slots[lo_i]
                    for j in rcs:
                        mm3(banks[j - rcs[0]][:],
                            hs[:, j * 128:(j + 1) * 128], ls[:, j * 128:(j + 1) * 128],
                            xfmh[:, c * TG:(c + 1) * TG], xfml[:, c * TG:(c + 1) * TG],
                            c == 0, c == NDC - 1)
                    pv = _inc_last(qs.tensor, pe)
                    ring.release(hi_i, pv)
                    ring.release(lo_i, pv)
                pvh = pe.v
                for j in rcs:
                    _w(qs.vector, pe, pvh)
                    qs.vector.append(lambda e, p=banks[j - rcs[0]], d=P[j]:
                                     e.tensor_copy(out=d[:], in_=p[:]))
                    dv = _inc_last(qs.vector, dve)
                    ps.set_free(j - rcs[0], dve, dv)

            for j in range(4, 10):
                qs.vector.append(lambda e, d=P[j], cj=j - 4:
                                 e.tensor_tensor_scan(out=d[:], data0=d[:], data1=d[:],
                                                      initial=carry[:, cj:cj + 1],
                                                      op0=ALU.add, op1=ALU.bypass))
                qs.vector.append(lambda e, d=P[j], cj=j - 4:
                                 e.tensor_copy(out=carry[:, cj:cj + 1],
                                               in_=d[:, TG - 1:TG]))
                _inc_last(qs.vector, dve)

            if "P" in dbg and g == 0:
                dvP = dve.v
                _w(qs.sync, dve, dvP)
                _w(qs.sync, dbg_sem, dbg_sem.v)
                for j in range(10):
                    qs.sync.append(lambda e, d=dbg["P"][j * 128:(j + 1) * 128, :],
                                   s=P[j]: e.dma_start(out=d, in_=s[:])
                                   .then_inc(dbg_sem.h, 16))
                    dbg_sem.bump(16)
                _w(qs.vector, dbg_sem, dbg_sem.v)

            for (a, b) in [(0, 4), (1, 5), (2, 8), (2, 6), (3, 9), (3, 7)]:
                qs.vector.append(lambda e, i0=a, i1=b:
                                 e.tensor_tensor(out=P[i0][:], in0=P[i0][:],
                                                 in1=P[i1][:], op=ALU.mult))
            dvp = _inc_last(qs.vector, dve)
            g_av, g_dv = [], []
            for j in range(4):
                _w(qs.scalar, dve, dvp)
                qs.scalar.append(lambda e, s=P[j], d=barena[j]:
                                 e.activation(out=d[:], in_=s[:], func=AF.Copy))
                av = _inc_last(qs.scalar, act)
                _w(qs.vector, act, av)
                qs.vector.append(lambda e, s=P[j], h=barena[j], d=barena[j + 4]:
                                 e.tensor_tensor(out=d[:], in0=s[:], in1=h[:],
                                                 op=ALU.subtract))
                dv = _inc_last(qs.vector, dve)
                g_av.append(av)
                g_dv.append(dv)


            cu = {}
            for part in range(2):
                for rc in range(2):
                    r0 = part * 256 + rc * 128
                    cu[(part, rc, 0)] = ring.load(cU_hi[r0:r0 + 128, :], D)
                    cu[(part, rc, 1)] = ring.load(cU_lo[r0:r0 + 128, :], D)

            tri_dv = [0] * NSUB
            bil_dv = [[0, 0] for _ in range(NSUB)]
            for part in range(2):
                banks = [ps.acquire(qs.tensor, i) for i in range(8)]
                if part == 0:
                    _w(qs.tensor, act, max(g_av))
                    _w(qs.tensor, dve, max(g_dv))
                for rc in range(2):
                    jch = part * 2 + rc
                    hi_i = cu[(part, rc, 0)]
                    lo_i = cu[(part, rc, 1)]
                    ring.wait_loaded(qs.tensor, hi_i)
                    ring.wait_loaded(qs.tensor, lo_i)
                    hs, ls = ring.slots[hi_i], ring.slots[lo_i]
                    for s_i in range(NSUB):
                        gh = barena[jch][:, s_i * 128:(s_i + 1) * 128]
                        gl = barena[jch + 4][:, s_i * 128:(s_i + 1) * 128]
                        for dh in range(2):
                            mm3(banks[s_i * 2 + dh][:], gh, gl,
                                hs[:, dh * 512:(dh + 1) * 512],
                                ls[:, dh * 512:(dh + 1) * 512], rc == 0, rc == 1)
                    pv = _inc_last(qs.tensor, pe)
                    ring.release(hi_i, pv)
                    ring.release(lo_i, pv)
                pvp = pe.v
                for s_i in range(NSUB):
                    ic_c = icnt[:, g * 4 + s_i:g * 4 + s_i + 1]
                    ic2_c = icnt2[:, g * 4 + s_i:g * 4 + s_i + 1]
                    for dh in range(2):
                        pb = banks[s_i * 2 + dh]
                        _w(qs.vector, pe, pvp)
                        if part == 0:
                            qs.vector.append(lambda e, p=pb, c=ic_c, s2=s_i, d2=dh:
                                             e.tensor_scalar(
                                                 out=x1[s2][:, d2 * 512:(d2 + 1) * 512],
                                                 in0=p[:], scalar1=c, scalar2=None,
                                                 op0=ALU.mult))
                            dv = _inc_last(qs.vector, dve)
                            bil_dv[s_i][dh] = dv
                        else:
                            _w(qs.vector, dve, bil_dv[s_i][dh])
                            qs.vector.append(lambda e, p=pb, c=ic2_c, s2=s_i, d2=dh:
                                             e.scalar_tensor_tensor(
                                                 out=x1[s2][:, d2 * 512:(d2 + 1) * 512],
                                                 in0=p[:], scalar=c,
                                                 in1=x1[s2][:, d2 * 512:(d2 + 1) * 512],
                                                 op0=ALU.mult, op1=ALU.add))
                            dv = _inc_last(qs.vector, dve)
                            tri_dv[s_i] = dv
                        ps.set_free(s_i * 2 + dh, dve, dv)

            ln1_dv = []
            for s_i in range(NSUB):
                i = s_i % 2
                sem = xtm_sem[i]
                if xtm_use[i]:
                    _w(qs.sync, gpc, xtm_use[i])
                v = sem.bump(16)
                qs.sync.append(lambda e, d=xtm_b[i],
                               s=x_tm[g * TG + s_i * 128:g * TG + (s_i + 1) * 128, :],
                               w=sem.h:
                               e.dma_start(out=d[:], in_=s).then_inc(w, 16))
                _w(qs.gpsimd, sem, v)
                if g == 0 and s_i == 0:
                    _w(qs.gpsimd, dve, v_pro)
                    _w(qs.gpsimd, dve, dvb2)
                if xb_used[i]:
                    _w(qs.gpsimd, dve, xb_used[i])
                qs.gpsimd.append(lambda e, s=xtm_b[i], d=xb[i]:
                                 e.tensor_tensor(out=d[:], in0=s[:], in1=bias_bc[:],
                                                 op=ALU.add))
                gv = _inc_last(qs.gpsimd, gpc)
                xtm_use[i] = gv
                _w(qs.vector, gpc, gv)
                _w(qs.vector, dve, tri_dv[s_i])
                qs.vector.append(lambda e, s2=s_i, i=s_i % 2:
                                 e.tensor_tensor(out=x1[s2][:], in0=x1[s2][:],
                                                 in1=xb[i][:], op=ALU.add))
                dv_r1 = _inc_last(qs.vector, dve)
                xb_used[s_i % 2] = dv_r1
                qs.vector.append(lambda e, s2=s_i:
                                 e.tensor_reduce(out=stats[:, s2:s2 + 1], in_=x1[s2][:],
                                                 axis=AX.X, op=ALU.add))
                dv = _inc_last(qs.vector, dve)
                _w(qs.scalar, dve, dv)
                qs.scalar.append(lambda e, s2=s_i:
                                 e.activation(out=sqs[:], in_=x1[s2][:], func=AF.Square,
                                              accum_out=stats[:, 4 + s2:5 + s2]))
                av = _inc_last(qs.scalar, act)
                _w(qs.vector, act, av)
                qs.vector.append(lambda e, s2=s_i:
                                 e.tensor_scalar(out=stats[:, s2:s2 + 1],
                                                 in0=stats[:, s2:s2 + 1],
                                                 scalar1=1.0 / D, scalar2=None,
                                                 op0=ALU.mult))
                qs.vector.append(lambda e, s2=s_i:
                                 e.tensor_tensor(out=stats[:, 8 + s2:9 + s2],
                                                 in0=stats[:, s2:s2 + 1],
                                                 in1=stats[:, s2:s2 + 1], op=ALU.mult))
                qs.vector.append(lambda e, s2=s_i:
                                 e.scalar_tensor_tensor(out=stats[:, 4 + s2:5 + s2],
                                                        in0=stats[:, 4 + s2:5 + s2],
                                                        scalar=1.0 / D,
                                                        in1=stats[:, 8 + s2:9 + s2],
                                                        op0=ALU.mult, op1=ALU.subtract))
                qs.vector.append(lambda e, s2=s_i:
                                 e.tensor_scalar(out=stats[:, 4 + s2:5 + s2],
                                                 in0=stats[:, 4 + s2:5 + s2],
                                                 scalar1=EPS, scalar2=None,
                                                 op0=ALU.add))
                qs.vector.append(lambda e, s2=s_i:
                                 e.reciprocal(out=stats[:, 8 + s2:9 + s2],
                                              in_=stats[:, 4 + s2:5 + s2]))
                dv2 = _inc_last(qs.vector, dve)
                _w(qs.scalar, dve, dv2)
                qs.scalar.append(lambda e, s2=s_i:
                                 e.activation(out=stats[:, 12 + s2:13 + s2],
                                              in_=stats[:, 8 + s2:9 + s2],
                                              func=AF.Sqrt))
                av2 = _inc_last(qs.scalar, act)
                _w(qs.vector, act, av2)
                qs.vector.append(lambda e, s2=s_i:
                                 e.tensor_scalar(out=x1[s2][:], in0=x1[s2][:],
                                                 scalar1=stats[:, s2:s2 + 1],
                                                 scalar2=stats[:, 12 + s2:13 + s2],
                                                 op0=ALU.subtract, op1=ALU.mult))
                dv3 = _inc_last(qs.vector, dve)
                ln1_dv.append(dv3)

            if "x1" in dbg and g == 0:
                _w(qs.sync, dbg_sem, dbg_sem.v)
                for s_i in range(NSUB):
                    _w(qs.sync, dve, ln1_dv[s_i])
                    qs.sync.append(lambda e, s2=s_i,
                                   d=dbg["x1"][s_i * 128:(s_i + 1) * 128, :]:
                                   e.dma_start(out=d, in_=x1[s2][:])
                                   .then_inc(dbg_sem.h, 16))
                    dbg_sem.bump(16)

            tr_av, tr_dv = [], []
            for s_i in range(NSUB):
                _w(qs.tensor, dve, ln1_dv[s_i])
                if g == 0 and s_i == 0:
                    _w(qs.tensor, sml_sem, sml_target)
                for c in range(NDC):
                    bank = 6 + (c % 2)
                    pb = ps.acquire(qs.tensor, bank)
                    qs.tensor.append(lambda e, p=pb, s2=s_i, c2=c:
                                     e.transpose(p[:, 0:128],
                                                 x1[s2][:, c2 * 128:(c2 + 1) * 128],
                                                 idn[:]))
                    pv = _inc_last(qs.tensor, pe)
                    col0 = c * TG + s_i * 128
                    _w(qs.scalar, pe, pv)
                    qs.scalar.append(lambda e, p=pb, c0=col0:
                                     e.activation(out=x1h[:, c0:c0 + 128],
                                                  in_=p[:, 0:128], func=AF.Copy))
                    av = _inc_last(qs.scalar, act)
                    _w(qs.vector, pe, pv)
                    _w(qs.vector, act, av)
                    qs.vector.append(lambda e, p=pb, c0=col0:
                                     e.tensor_tensor(out=x1l[:, c0:c0 + 128],
                                                     in0=p[:, 0:128],
                                                     in1=x1h[:, c0:c0 + 128],
                                                     op=ALU.subtract))
                    dv = _inc_last(qs.vector, dve)
                    ps.set_free(bank, dve, dv)
                    tr_av.append(av)
                    tr_dv.append(dv)

            if stage == "attn":
                continue
            pb = ps.acquire(qs.tensor, 0)
            _w(qs.tensor, act, max(tr_av))
            _w(qs.tensor, dve, max(tr_dv))
            for c in range(NDC):
                mm3(pb[0:8, :], gwh[:, c * 8:(c + 1) * 8], gwl[:, c * 8:(c + 1) * 8],
                    x1h[:, c * TG:(c + 1) * TG], x1l[:, c * TG:(c + 1) * TG],
                    c == 0, c == NDC - 1)
            pv = _inc_last(qs.tensor, pe)
            _w(qs.vector, pe, pv)
            qs.vector.append(lambda e, p=pb:
                             e.tensor_copy(out=lgfm[:], in_=p[0:8, :]))
            dv_lg = _inc_last(qs.vector, dve)
            ps.set_free(0, dve, dv_lg)
            gate_done = []
            for s_i in range(NSUB):
                bank = 1 + (s_i % 2)
                pb2 = ps.acquire(qs.tensor, bank)
                _w(qs.tensor, dve, dv_lg)
                qs.tensor.append(lambda e, p=pb2, s2=s_i:
                                 e.transpose(p[:, 0:8],
                                             lgfm[:, s2 * 128:(s2 + 1) * 128],
                                             idn[0:8, 0:8]))
                pv2 = _inc_last(qs.tensor, pe)
                c0 = s_i * 8
                c1 = 32 + s_i * 8
                _w(qs.vector, pe, pv2)
                qs.vector.append(lambda e, p=pb2:
                                 e.tensor_reduce(out=stats[:, 0:1], in_=p[:, 0:8],
                                                 axis=AX.X, op=ALU.max, negate=True))
                dv = _inc_last(qs.vector, dve)
                _w(qs.scalar, dve, dv)
                qs.scalar.append(lambda e, p=pb2, c=c0:
                                 e.activation(out=expv[:, c:c + 8], in_=p[:, 0:8],
                                              func=AF.Exp, bias=stats[:, 0:1],
                                              accum_out=stats[:, 1:2]))
                av = _inc_last(qs.scalar, act)
                ps.set_free(bank, act, av)
                _w(qs.vector, act, av)
                qs.vector.append(lambda e:
                                 e.reciprocal(out=stats[:, 2:3], in_=stats[:, 1:2]))
                dvr = _inc_last(qs.vector, dve)
                _w(qs.vector, dve, dvr)
                qs.vector.append(lambda e, c=c0:
                                 e.tensor_scalar(out=expv[:, c:c + 8],
                                                 in0=expv[:, c:c + 8],
                                                 scalar1=stats[:, 2:3], scalar2=None,
                                                 op0=ALU.mult))
                qs.vector.append(lambda e, c=c0, cc=c1:
                                 e.tensor_tensor(out=expv[:, cc:cc + 8],
                                                 in0=expv[:, c:c + 8], in1=gbias[:],
                                                 op=ALU.add))
                qs.vector.append(lambda e, cc=c1:
                                 e.tensor_reduce(out=stats[:, 3:4],
                                                 in_=expv[:, cc:cc + 8],
                                                 axis=AX.X, op=ALU.max))
                dvm = _inc_last(qs.vector, dve)
                _w(qs.vector, dve, dvm)
                qs.vector.append(lambda e, cc=c1, c=c0:
                                 e.tensor_scalar(out=gates[:, c:c + 8],
                                                 in0=expv[:, cc:cc + 8],
                                                 scalar1=stats[:, 3:4], scalar2=None,
                                                 op0=ALU.is_ge))
                qs.vector.append(lambda e, cc=c1, c=c0:
                                 e.scalar_tensor_tensor(out=gates[:, c:c + 8],
                                                        in0=gates[:, c:c + 8],
                                                        scalar=-1e9,
                                                        in1=expv[:, cc:cc + 8],
                                                        op0=ALU.mult, op1=ALU.add))
                qs.vector.append(lambda e, c=c0:
                                 e.tensor_reduce(out=stats[:, 3:4],
                                                 in_=gates[:, c:c + 8],
                                                 axis=AX.X, op=ALU.max))
                dvm2 = _inc_last(qs.vector, dve)
                _w(qs.vector, dve, dvm2)
                qs.vector.append(lambda e, cc=c1, c=c0:
                                 e.tensor_scalar(out=gates[:, c:c + 8],
                                                 in0=expv[:, cc:cc + 8],
                                                 scalar1=stats[:, 3:4], scalar2=None,
                                                 op0=ALU.is_ge))
                qs.vector.append(lambda e, c=c0:
                                 e.tensor_tensor(out=gates[:, c:c + 8],
                                                 in0=gates[:, c:c + 8],
                                                 in1=expv[:, c:c + 8], op=ALU.mult))
                dvg = _inc_last(qs.vector, dve)
                gate_done.append(dvg)

            if "gates" in dbg and g == 0:
                _w(qs.sync, dve, max(gate_done))
                _w(qs.sync, dbg_sem, dbg_sem.v)
                for s_i in range(NSUB):
                    qs.sync.append(lambda e, s2=s_i,
                                   d=dbg["gates"][s_i * 128:(s_i + 1) * 128, :]:
                                   e.dma_start(out=d, in_=gates[:, s2 * 8:s2 * 8 + 8])
                                   .then_inc(dbg_sem.h, 16))
                    dbg_sem.bump(16)

            if stage == "gate":
                continue
            ffn_dv_last = 0
            h_guard = (0, 0)
            for ex in range(NEXP):
                pieces = {}

                def load_up_pc(pc, _p=pieces, _ex=ex):
                    for m_i in range(2):
                        for hl in range(2):
                            scr = [wS_hi, wS_lo][hl]
                            r0 = row(_ex, m_i)
                            srcap = scr[r0:r0 + D, pc * 256:(pc + 1) * 256]
                            srcap = srcap.rearrange("(c p) i -> p c i", p=128)
                            _p[(m_i, hl, pc)] = ring.load(srcap, SLOTW, split=8)

                def load_w2_pc(pc, _p=pieces, _ex=ex):
                    for hl in range(2):
                        scr = [wS_hi, wS_lo][hl]
                        r0 = row(_ex, 2) + pc * 256
                        srcap = scr[r0:r0 + 256, :].rearrange("(c p) i -> p c i",
                                                              p=128)
                        _p[(2, hl, pc)] = ring.load(srcap, SLOTW, split=2)

                load_up_pc(0)
                load_up_pc(1)

                def upsl(m_i, hl, dchunk, ic):
                    i = pieces[(m_i, hl, ic // 2)]
                    base = dchunk * 256 + (ic % 2) * 128
                    return ring.slots[i][:, base:base + 128]

                def w2sl(hl, ic, c0, c1):
                    i = pieces[(2, hl, ic // 2)]
                    base = (ic % 2) * 1024
                    return ring.slots[i][:, base + c0:base + c1]

                h_av, h_dv = [], []
                for ic in range(8):
                    bank1 = (ic % 2) * 2
                    bank3 = bank1 + 1
                    pb1 = ps.acquire(qs.tensor, bank1)
                    pb3 = ps.acquire(qs.tensor, bank3)
                    if ic % 2 == 0:
                        for m_i in range(2):
                            for hl in range(2):
                                ring.wait_loaded(qs.tensor, pieces[(m_i, hl, ic // 2)])
                    if ic == 0:
                        _w(qs.tensor, dve, max(tr_dv))
                        _w(qs.tensor, act, max(tr_av))
                    for m_i, pbm in [(0, pb1), (1, pb3)]:
                        for c in range(NDC):
                            mm3(pbm[:], upsl(m_i, 0, c, ic), upsl(m_i, 1, c, ic),
                                x1h[:, c * TG:(c + 1) * TG],
                                x1l[:, c * TG:(c + 1) * TG], c == 0, c == NDC - 1)
                        _inc_last(qs.tensor, pe)
                    pvu = pe.v
                    if ic % 2 == 1:
                        pc = ic // 2
                        for m_i in range(2):
                            for hl in range(2):
                                ring.release(pieces[(m_i, hl, pc)], pvu)
                        if pc + 2 <= 3:
                            load_up_pc(pc + 2)
                        if pc == 1:
                            load_w2_pc(0)
                            load_w2_pc(1)
                        elif pc == 2:
                            load_w2_pc(2)
                            load_w2_pc(3)
                    _w(qs.scalar, pe, pvu)
                    if ic >= 1:
                        _w(qs.scalar, dve, h_dv[-1] if h_dv else 0)
                    qs.scalar.append(lambda e, p=pb1:
                                     e.activation(out=silu[:], in_=p[:], func=AF.Silu))
                    av = _inc_last(qs.scalar, act)
                    ps.set_free(bank1, act, av)
                    _w(qs.vector, pe, pvu)
                    _w(qs.vector, act, av)
                    qs.vector.append(lambda e, p=pb3:
                                     e.tensor_tensor(out=P[4][:], in0=silu[:],
                                                     in1=p[:], op=ALU.mult))
                    dv1 = _inc_last(qs.vector, dve)
                    ps.set_free(bank3, dve, dv1)
                    _w(qs.scalar, dve, dv1)
                    qs.scalar.append(lambda e, d=barena[ic]:
                                     e.activation(out=d[:], in_=P[4][:], func=AF.Copy))
                    av2 = _inc_last(qs.scalar, act)
                    _w(qs.vector, act, av2)
                    qs.vector.append(lambda e, d=barena[ic + 8], hh=barena[ic]:
                                     e.tensor_tensor(out=d[:], in0=P[4][:],
                                                     in1=hh[:],
                                                     op=ALU.subtract))
                    dv2 = _inc_last(qs.vector, dve)
                    h_av.append(av2)
                    h_dv.append(dv2)

                if "h0" in dbg and g == 0 and ex == 0:
                    _w(qs.sync, act, max(h_av))
                    _w(qs.sync, dve, max(h_dv))
                    _w(qs.sync, dbg_sem, dbg_sem.v)
                    for jj in range(16):
                        qs.sync.append(lambda e, d=dbg["h0"][jj * 128:(jj + 1) * 128, :],
                                       s=barena[jj]: e.dma_start(out=d, in_=s[:])
                                       .then_inc(dbg_sem.h, 16))
                        dbg_sem.bump(16)
                        if jj % 4 == 3:
                            _w(qs.sync, dbg_sem, dbg_sem.v)
                    _w(qs.tensor, dbg_sem, dbg_sem.v)
                for s_i in range(NSUB):
                    bank = 4 + (s_i % 2) * 2
                    pbA = ps.acquire(qs.tensor, bank)
                    pbB = ps.acquire(qs.tensor, bank + 1)
                    if s_i == 0:
                        for hl in range(2):
                            for pc in range(4):
                                ring.wait_loaded(qs.tensor, pieces[(2, hl, pc)])
                        _w(qs.tensor, act, max(h_av))
                        _w(qs.tensor, dve, max(h_dv))
                    for dh, pbm in [(0, pbA), (1, pbB)]:
                        for ic in range(8):
                            mm3(pbm[:], barena[ic][:, s_i * 128:(s_i + 1) * 128],
                                barena[ic + 8][:, s_i * 128:(s_i + 1) * 128],
                                w2sl(0, ic, dh * 512, (dh + 1) * 512),
                                w2sl(1, ic, dh * 512, (dh + 1) * 512),
                                ic == 0, ic == 7)
                        _inc_last(qs.tensor, pe)
                    pvd = pe.v
                    for dh, pbm, bk in [(0, pbA, bank), (1, pbB, bank + 1)]:
                        _w(qs.vector, pe, pvd)
                        if ex >= 2:
                            if ex == 2 and s_i == 0 and dh == 0:
                                _w(qs.vector, dve, max(gate_done))
                            qs.vector.append(lambda e, p=pbm, s2=s_i, d2=dh, x=ex:
                                             e.scalar_tensor_tensor(
                                                 out=x1[s2][:, d2 * 512:(d2 + 1) * 512],
                                                 in0=p[:],
                                                 scalar=gates[:, s2 * 8 + x - 2:
                                                              s2 * 8 + x - 1],
                                                 in1=x1[s2][:, d2 * 512:(d2 + 1) * 512],
                                                 op0=ALU.mult, op1=ALU.add))
                        else:
                            qs.vector.append(lambda e, p=pbm, s2=s_i, d2=dh:
                                             e.tensor_tensor(
                                                 out=x1[s2][:, d2 * 512:(d2 + 1) * 512],
                                                 in0=x1[s2][:, d2 * 512:(d2 + 1) * 512],
                                                 in1=p[:], op=ALU.add))
                        dv = _inc_last(qs.vector, dve)
                        ps.set_free(bk, dve, dv)
                        ffn_dv_last = dv
                for key, idx in pieces.items():
                    if key[0] == 2:
                        ring.release(idx, pe.v)
                if "acc" in dbg and g == 0:
                    _w(qs.sync, dve, ffn_dv_last)
                    _w(qs.sync, dbg_sem, dbg_sem.v)
                    for s_i in range(NSUB):
                        qs.sync.append(lambda e, s2=s_i,
                                       d=dbg["acc"][ex * TG + s_i * 128:
                                                    ex * TG + (s_i + 1) * 128, :]:
                                       e.dma_start(out=d, in_=x1[s2][:])
                                       .then_inc(dbg_sem.h, 16))
                        dbg_sem.bump(16)
                    _w(qs.vector, dbg_sem, dbg_sem.v)

            if "r2" in dbg and g == 0:
                _w(qs.sync, dve, ffn_dv_last)
                _w(qs.sync, dbg_sem, dbg_sem.v)
                for s_i in range(NSUB):
                    qs.sync.append(lambda e, s2=s_i,
                                   d=dbg["r2"][s_i * 128:(s_i + 1) * 128, :]:
                                   e.dma_start(out=d, in_=x1[s2][:])
                                   .then_inc(dbg_sem.h, 16))
                    dbg_sem.bump(16)

            for s_i in range(NSUB):
                _w(qs.vector, dve, ffn_dv_last)
                qs.vector.append(lambda e, s2=s_i:
                                 e.tensor_reduce(out=stats[:, s2:s2 + 1], in_=x1[s2][:],
                                                 axis=AX.X, op=ALU.add))
                dv = _inc_last(qs.vector, dve)
                _w(qs.scalar, dve, dv)
                qs.scalar.append(lambda e, s2=s_i:
                                 e.activation(out=sqs[:], in_=x1[s2][:], func=AF.Square,
                                              accum_out=stats[:, 4 + s2:5 + s2]))
                av = _inc_last(qs.scalar, act)
                _w(qs.vector, act, av)
                qs.vector.append(lambda e, s2=s_i:
                                 e.tensor_scalar(out=stats[:, s2:s2 + 1],
                                                 in0=stats[:, s2:s2 + 1],
                                                 scalar1=1.0 / D, scalar2=None,
                                                 op0=ALU.mult))
                qs.vector.append(lambda e, s2=s_i:
                                 e.tensor_tensor(out=stats[:, 8 + s2:9 + s2],
                                                 in0=stats[:, s2:s2 + 1],
                                                 in1=stats[:, s2:s2 + 1], op=ALU.mult))
                qs.vector.append(lambda e, s2=s_i:
                                 e.scalar_tensor_tensor(out=stats[:, 4 + s2:5 + s2],
                                                        in0=stats[:, 4 + s2:5 + s2],
                                                        scalar=1.0 / D,
                                                        in1=stats[:, 8 + s2:9 + s2],
                                                        op0=ALU.mult, op1=ALU.subtract))
                qs.vector.append(lambda e, s2=s_i:
                                 e.tensor_scalar(out=stats[:, 4 + s2:5 + s2],
                                                 in0=stats[:, 4 + s2:5 + s2],
                                                 scalar1=EPS, scalar2=None,
                                                 op0=ALU.add))
                qs.vector.append(lambda e, s2=s_i:
                                 e.reciprocal(out=stats[:, 8 + s2:9 + s2],
                                              in_=stats[:, 4 + s2:5 + s2]))
                dv2 = _inc_last(qs.vector, dve)
                _w(qs.scalar, dve, dv2)
                qs.scalar.append(lambda e, s2=s_i:
                                 e.activation(out=stats[:, 12 + s2:13 + s2],
                                              in_=stats[:, 8 + s2:9 + s2],
                                              func=AF.Sqrt))
                av2 = _inc_last(qs.scalar, act)
                _w(qs.vector, act, av2)
                i = s_i % 2
                if out_use[i]:
                    _w(qs.vector, st_sem[i], out_use[i])
                qs.vector.append(lambda e, s2=s_i, i2=i:
                                 e.tensor_scalar(out=outb[i2][:], in0=x1[s2][:],
                                                 scalar1=stats[:, s2:s2 + 1],
                                                 scalar2=stats[:, 12 + s2:13 + s2],
                                                 op0=ALU.subtract, op1=ALU.mult))
                dv3 = _inc_last(qs.vector, dve)
                _w(qs.gpsimd, dve, dv3)
                qs.gpsimd.append(lambda e, i2=i,
                                 d=out[g * TG + s_i * 128:g * TG + (s_i + 1) * 128, :],
                                 w=st_sem[i].h:
                                 e.dma_start(out=d, in_=outb[i2][:])
                                 .then_inc(w, 16))
                out_use[i] = st_sem[i].bump(16)

        qs.gpsimd.append(lambda e: e.wait_ge(st_sem[0].h, st_sem[0].v))
        qs.gpsimd.append(lambda e: e.wait_ge(st_sem[1].h, st_sem[1].v))
        if dbg_sem.v:
            qs.sync.append(lambda e: e.wait_ge(dbg_sem.h, dbg_sem.v))

        with nc.Block() as block:
            @block.sync
            def _(e):
                for fn in qs.sync:
                    fn(e)

            @block.tensor
            def _(e):
                for fn in qs.tensor:
                    fn(e)

            @block.vector
            def _(e):
                for fn in qs.vector:
                    fn(e)

            @block.scalar
            def _(e):
                for fn in qs.scalar:
                    fn(e)

            @block.gpsimd
            def _(e):
                for fn in qs.gpsimd:
                    fn(e)

    return nc, dbg


def host_inputs(inputs):
    f32 = np.float32
    C = lambda a: np.ascontiguousarray(np.asarray(a, f32))
    x = np.asarray(inputs["L0_attn_x"], f32)
    Wq = C(inputs["L0_attn_Wq"]); Wk = C(inputs["L0_attn_Wk"])
    WoT = C(np.asarray(inputs["L0_attn_Wo"], f32).T)
    V_b = C(inputs["L0_attn_V_b"]); V_t = C(inputs["L0_attn_V_t"])
    W_b = C(inputs["L0_attn_W_b"]); W_t = C(inputs["L0_attn_W_t"])
    X_t = C(inputs["L0_attn_X_t"])
    U_b = C(inputs["L0_attn_U_b"]); U_t = C(inputs["L0_attn_U_t"])
    bias_b = np.asarray(inputs["L0_attn_bias_b"], f32)
    bias_t = np.asarray(inputs["L0_attn_bias_t"], f32)
    alpha = f32(np.asarray(inputs["L0_attn_alpha"], f32))
    gw = np.asarray(inputs["L0_ffn_gate_w"], f32)
    gbias = np.asarray(inputs["L0_ffn_gate_bias"], f32)
    w1 = np.asarray(inputs["L0_ffn_w1"], f32)
    w3 = np.asarray(inputs["L0_ffn_w3"], f32)
    w2 = np.asarray(inputs["L0_ffn_w2"], f32)
    sw1 = np.asarray(inputs["L0_ffn_sw1"], f32)
    sw3 = np.asarray(inputs["L0_ffn_sw3"], f32)
    sw2 = np.asarray(inputs["L0_ffn_sw2"], f32)

    shared = dict(
        Wq=Wq, Wk=Wk, WoT=WoT, V_b=V_b, V_t=V_t, W_b=W_b, W_t=W_t, X_t=X_t,
        U_b=U_b, U_t=U_t,
        bias_cols=np.ascontiguousarray(np.concatenate(
            [bias_b.reshape(8, 128).T, bias_t.reshape(8, 128).T], axis=1)),
        alpha_col=np.full((128, 1), alpha, f32),
        gwT=C(gw.T),
        gbias_bc=np.ascontiguousarray(np.broadcast_to(gbias[None, :], (128, 8))),
        w1T=C(w1.transpose(0, 2, 1).reshape(8 * D, D)),
        w3T=C(w3.transpose(0, 2, 1).reshape(8 * D, D)),
        w2T=C(w2.transpose(0, 2, 1).reshape(8 * D, D)),
        sw1T=C(sw1.T), sw3T=C(sw3.T), sw2T=C(sw2.T),
        ident=np.eye(128, dtype=f32),
        counts=np.ascontiguousarray(np.arange(T, dtype=f32).reshape(32, 128).T + 1.0),
    )
    in_maps = []
    for b in range(8):
        m = dict(shared)
        m["x_tm"] = np.ascontiguousarray(x[b])
        m["x_fm"] = np.ascontiguousarray(x[b].T)
        in_maps.append(m)
    return in_maps


def kernel(**inputs):
    in_maps = host_inputs(inputs)
    nc, _ = build_program()
    res = run_bass_kernel_spmd(nc, in_maps, core_ids=list(range(8)))
    return np.stack([np.asarray(res.results[b]["out"]) for b in range(8)], axis=0)
